# version 01
# baseline (speedup 1.0000x reference)
import contextlib
import os
import numpy as np
import ml_dtypes
import concourse.bass as bass
import concourse.mybir as mybir
from concourse.bass import IndirectOffsetOnAxis, ds
from concourse.bass_utils import run_bass_kernel_spmd

F32 = mybir.dt.float32
BF16 = mybir.dt.bfloat16
I32 = mybir.dt.int32
AF = mybir.ActivationFunctionType
ALU = mybir.AluOpType
AX = mybir.AxisListType
BF = ml_dtypes.bfloat16

B_GLOBAL, L, D = 16, 2048, 2048
NCORE = 8
B_LOC = 2
NTOK = B_LOC * L
D_INNER, D_XB, D_STATE, D_CONV, DT_RANK = 4096, 512, 16, 4, 128
NBLK = 32
HID, NE, TOPK = 1408, 8, 2
NENT = TOPK * NTOK
GRP = 1024
NGRP = 16
NROWS = NGRP * GRP
KC = 16
HCN = 11
EPS = 1e-5
DEBUG = False
UPTO = 6
DBG_SCAN = False
NGRP_RUN_OVERRIDE = None
ST5 = 5


class Cnt:
    def __init__(self, handle):
        self.h = handle
        self.v = 0

    def inc(self, instr, k=1):
        instr.then_inc(self.h, k)
        self.v += k
        return self.v


def build_nc():
    nc = bass.Bass("TRN2", target_bir_lowering=False)

    def din(name, shape, dt=F32):
        return nc.dram_tensor(name, shape, dt, kind="ExternalInput").ap()

    x_tok = din("x_tok", [NTOK, D])
    W_inT = din("W_inT", [D, 9344])
    W_dtT = din("W_dtT", [DT_RANK, D_INNER])
    b_dt_l = din("b_dt_l", [128, NBLK])
    conv_w_l = din("conv_w_l", [128, NBLK * D_CONV])
    conv_b_l = din("conv_b_l", [128, NBLK])
    A_log_l = din("A_log_l", [128, NBLK * D_STATE])
    D_l = din("D_l", [128, NBLK])
    W_outT = din("W_outT", [D_INNER, D])
    gate_wT = din("gate_wT", [D, NE])
    w1T = din("w1T", [NE * D, HID])
    w3T = din("w3T", [NE * D, HID])
    w2T = din("w2T", [NE * HID, D])
    sw1T = din("sw1T", [D, HID])
    sw3T = din("sw3T", [D, HID])
    sw2T = din("sw2T", [HID, D])
    ebias_tm = din("ebias_tm", [128, NE])
    identb = din("identb", [128, 128], BF16)
    identf = din("identf", [128, 128])
    selB = din("selB", [16, 16 * 128], BF16)
    selC_in = din("selC_in", [128, 16 * 128], BF16)
    selX = din("selX", [16, 128])
    ones8b = din("ones8b", [8, 1], BF16)
    lt8 = din("lt8", [8, 8], BF16)
    ones1x128 = din("ones1x128", [1, 128], BF16)
    iota8col = din("iota8col", [8, 1])
    iota8tm = din("iota8tm", [128, NE])
    big_tm = din("big_tm", [128, NE])
    iotam99 = din("iotam99", [128, NE])
    neginf_tm = din("neginf_tm", [128, NE])
    thr17 = din("thr17", [8, 17])
    gthr = din("gthr", [8, NGRP])
    tokid_i = din("tokid_i", [128, 64], I32)
    fill_oob = din("fill_oob", [128, 128], I32)
    zeros128 = din("zeros128", [128, 128])
    ones8x = din("ones8x", [8, NENT])
    zeros_bf_row = din("zeros_bf_row", [1, D], BF16)
    constvals = din("constvals", [128, 4])
    out_tok = nc.dram_tensor("out_tok", [NTOK, D], F32, kind="ExternalOutput").ap()

    def scr(name, shape, dt, dbgout=False):
        kind = "ExternalOutput" if (DEBUG and dbgout) else "Internal"
        return nc.dram_tensor(name, shape, dt, kind=kind).ap()

    xn_tok_bf = scr("xn_tok_bf", [NTOK + 1, D], BF16, True)
    xn_fm_bf = scr("xn_fm_bf", [D, NTOK], BF16, True)
    z_bf = scr("z_bf", [D_INNER, NTOK], BF16, True)
    xraw_f = scr("xraw_f", [D_XB, NTOK], F32, True)
    bm_bf = scr("bm_bf", [D_XB, NTOK], BF16, True)
    cm_bf = scr("cm_bf", [D_INNER, NTOK], BF16, True)
    dtl_bf = scr("dtl_bf", [DT_RANK, NTOK], BF16, True)
    ym_bf = scr("ym_bf", [D_INNER, NTOK], BF16, True)
    hm_bf = scr("hm_bf", [HID, NTOK], BF16, True)
    w1t_bf = scr("w1t_bf", [NE * D, HID], BF16)
    w3t_bf = scr("w3t_bf", [NE * D, HID], BF16)
    w2t_bf = scr("w2t_bf", [NE * HID, D], BF16)
    winT_bf = scr("winT_bf", [D, 9344], BF16)
    woutT_bf = scr("woutT_bf", [D_INNER, D], BF16)
    sw1t_bf = scr("sw1t_bf", [D, HID], BF16)
    sw3t_bf = scr("sw3t_bf", [D, HID], BF16)
    sw2t_bf = scr("sw2t_bf", [HID, D], BF16)
    gwt_bf = scr("gwt_bf", [D, NE], BF16)
    so_bf = scr("so_bf", [NROWS, D], BF16, True)
    tok_tab = scr("tok_tab", [NROWS, 1], I32, True)
    sco_tab = scr("sco_tab", [NROWS, 1], F32, True)
    eksv = scr("eksv", [2, NENT], F32, True)
    pos_dram = scr("pos_dram", [NENT], F32, True)
    s_tm_f = scr("s_tm_f", [NTOK, D], F32, True)
    w1stg = scr("w1stg", [2, D, HID], BF16)
    w3stg = scr("w3stg", [2, D, HID], BF16)
    w2stg = scr("w2stg", [2, HID, D], BF16)
    dbg_scan = scr("dbg_scan", [6 * 128, 2048], F32, DBG_SCAN)
    dbg_tei = scr("dbg_tei", [1, NGRP], I32, True)

    st = contextlib.ExitStack()

    def sbuf(name, shape, dt=F32, stk=None):
        return (stk or st).enter_context(nc.sbuf_tensor(name, shape, dt))

    def psum(name, shape, dt=F32, stk=None):
        return (stk or st).enter_context(nc.psum_tensor(name, shape, dt))

    def sem(name):
        return Cnt(st.enter_context(nc.semaphore(name)))

    c_identb = sbuf("c_identb", [128, 128], BF16)
    c_identf = sbuf("c_identf", [128, 128])
    c_selB = sbuf("c_selB", [16, 16 * 128], BF16)
    c_selC = sbuf("c_selC", [128, 16 * 128], BF16)
    c_selX = sbuf("c_selX", [16, 128])
    c_ones8b = sbuf("c_ones8b", [8, 1], BF16)
    c_lt8 = sbuf("c_lt8", [8, 8], BF16)
    c_ones1x128 = sbuf("c_ones1x128", [1, 128], BF16)
    c_iota8col = sbuf("c_iota8col", [8, 1])
    c_iota8tm = sbuf("c_iota8tm", [128, NE])
    c_big = sbuf("c_big", [128, NE])
    c_iotam99 = sbuf("c_iotam99", [128, NE])
    c_neginf = sbuf("c_neginf", [128, NE])
    c_thr17 = sbuf("c_thr17", [8, 17])
    c_gthr = sbuf("c_gthr", [8, NGRP])
    c_tokid = sbuf("c_tokid", [128, 64], I32)
    c_ebias = sbuf("c_ebias", [128, NE])
    c_zeros = sbuf("c_zeros", [128, 128])
    c_cval = sbuf("c_cval", [128, 4])
    nc.const_aps.register(F32, EPS, c_cval[:, 0:1])
    nc.const_aps.register(F32, -0.5, c_cval[:, 1:2])
    nc.const_aps.register(F32, -1.0, c_cval[:, 2:3])
    p_b2 = sbuf("p_b2", [128, NBLK])
    p_cw = sbuf("p_cw", [128, NBLK * D_CONV])
    p_cb = sbuf("p_cb", [128, NBLK])
    p_A = sbuf("p_A", [128, NBLK * D_STATE])
    p_D = sbuf("p_D", [128, NBLK])
    p_wdt = sbuf("p_wdt", [DT_RANK, D_INNER], BF16)
    rowidx = sbuf("rowidx", [128, NROWS // 128], I32)
    scorow = sbuf("scorow", [128, NROWS // 128])
    posi = sbuf("posi", [128, 64], I32)
    tei_i32 = sbuf("tei_i32", [1, NGRP], I32)
    tei2048 = sbuf("tei2048", [1, NGRP], I32)
    tei1408 = sbuf("tei1408", [1, NGRP], I32)
    gsc_sb = sbuf("gsc_sb", [8, NTOK])

    dl = sem("dl")
    dv = sem("dv")
    dsq = sem("dsq")
    dtq = sem("dtq")
    dgq = sem("dgq")
    ct = sem("ct")
    cv = sem("cv")
    csc = sem("csc")
    cgp = sem("cgp")
    t4b = sem("t4b")
    t4c = sem("t4c")
    ev2 = sem("ev2")
    vch = sem("vch")
    sch = sem("sch")

    def vchain(v, instr):
        vch.inc(instr)
        v.wait_ge(vch.h, vch.v)
        return instr

    def schain(s_, instr):
        sch.inc(instr)
        s_.wait_ge(sch.h, sch.v)
        return instr

    blk = st.enter_context(nc.Block())
    ALLC = [dl, dv, dsq, dtq, dgq, ct, cv, csc, cgp, t4b, t4c, ev2, vch, sch]

    def snap():
        return {c: c.v for c in ALLC}

    def wait_snap(eng, sn):
        for c, v in sn.items():
            if v > 0:
                eng.wait_ge(c.h, v)

    def p0a_sync(q):
        for dst, src in [
            (c_identb, identb), (c_identf, identf), (c_selB, selB),
            (c_selC, selC_in), (c_selX, selX), (c_ones8b, ones8b),
            (c_lt8, lt8), (c_ones1x128, ones1x128), (c_iota8col, iota8col),
            (c_iota8tm, iota8tm), (c_big, big_tm), (c_neginf, neginf_tm),
            (c_iotam99, iotam99),
            (c_thr17, thr17), (c_gthr, gthr),
            (c_tokid, tokid_i), (c_ebias, ebias_tm), (c_zeros, zeros128),
            (c_cval, constvals),
            (p_b2, b_dt_l), (p_cw, conv_w_l), (p_cb, conv_b_l),
            (p_A, A_log_l), (p_D, D_l),
        ]:
            dl.inc(q.dma_start(out=dst[:], in_=src), 16)
        dl.inc(q.dma_start(out=xn_tok_bf[NTOK:NTOK + 1, :], in_=zeros_bf_row), 16)
    blk.sync(p0a_sync)
    P0A = dl.v

    def p0a_vec(v):
        v.wait_ge(dl.h, P0A)
        cv.inc(v.tensor_add(p_b2[:], p_b2[:], p_b2[:]))
    blk.vector(p0a_vec)

    def p0a_sca(s):
        s.wait_ge(dl.h, P0A)
        schain(s, s.activation(p_A[:], p_A[:], AF.Exp))
        csc.inc(s.mul(p_A[:], p_A[:], -1.0))
    blk.scalar(p0a_sca)

    jobs = []

    def add_jobs(src, dst_dram, dst_sbuf, nrows, ncols):
        for r0 in range(0, nrows, 128):
            rn = min(128, nrows - r0)
            for c0 in range(0, ncols, 2048):
                cn = min(2048, ncols - c0)
                jobs.append((src, dst_dram, dst_sbuf, r0, rn, c0, cn))

    add_jobs(W_dtT, None, p_wdt, DT_RANK, D_INNER)
    add_jobs(gate_wT, gwt_bf, None, D, NE)
    add_jobs(W_inT, winT_bf, None, D, 9344)
    J_WIN = len(jobs) - 1
    add_jobs(sw1T, sw1t_bf, None, D, HID)
    add_jobs(sw3T, sw3t_bf, None, D, HID)
    J_SW13 = len(jobs) - 1
    add_jobs(w1T, w1t_bf, None, NE * D, HID)
    add_jobs(w3T, w3t_bf, None, NE * D, HID)
    add_jobs(w2T, w2t_bf, None, NE * HID, D)
    J_ROUTED = len(jobs) - 1
    add_jobs(sw2T, sw2t_bf, None, HID, D)
    J_SW2 = len(jobs) - 1
    add_jobs(W_outT, woutT_bf, None, D_INNER, D)
    NJOB = len(jobs)

    store_cnt = {}
    ns = 0
    for j, jb in enumerate(jobs):
        if jb[1] is not None:
            ns += 1
        store_cnt[j] = ns

    p0s = contextlib.ExitStack()
    cvt_f = [sbuf(f"cvt_f{i}", [128, 2048], F32, p0s) for i in range(2)]
    cvt_b = [sbuf(f"cvt_b{i}", [128, 2048], BF16, p0s) for i in range(2)]
    CV0, DV0, DL0 = cv.v, dv.v, dl.v

    def p0b_sync(q):
        for j, (src, dd, dsb, r0, rn, c0, cn) in enumerate(jobs):
            if j >= 2:
                q.wait_ge(cv.h, CV0 + j - 1)
            dl.inc(q.dma_start(out=cvt_f[j % 2][0:rn, 0:cn],
                               in_=src[r0:r0 + rn, c0:c0 + cn]), 16)
    blk.sync(p0b_sync)

    DSQ0 = dsq.v

    def p0b_vec(v):
        for j, (src, dd, dsb, r0, rn, c0, cn) in enumerate(jobs):
            v.wait_ge(dl.h, DL0 + 16 * (j + 1))
            if dsb is not None:
                cv.inc(v.tensor_copy(dsb[0:rn, c0:c0 + cn], cvt_f[j % 2][0:rn, 0:cn]))
                continue
            if j >= 2 and store_cnt[j - 2] > 0:
                v.wait_ge(dsq.h, DSQ0 + 16 * store_cnt[j - 2])
            cv.inc(v.tensor_copy(cvt_b[j % 2][0:rn, 0:cn], cvt_f[j % 2][0:rn, 0:cn]))
    blk.vector(p0b_vec)

    def p0b_sca(sc):
        for j, (src, dd, dsb, r0, rn, c0, cn) in enumerate(jobs):
            if dd is None:
                continue
            sc.wait_ge(cv.h, CV0 + j + 1)
            dsq.inc(sc.dma_start(out=dd[r0:r0 + rn, c0:c0 + cn],
                                 in_=cvt_b[j % 2][0:rn, 0:cn]), 16)
    blk.scalar(p0b_sca)
    WDT_READY = CV0 + 2
    WIN_READY = DSQ0 + 16 * store_cnt[J_WIN]
    ROUTED_READY = DSQ0 + 16 * store_cnt[J_ROUTED]
    ALLW_READY = DSQ0 + 16 * store_cnt[NJOB - 1]
    SN_P0 = snap()
    p0s.close()

    p1s = contextlib.ExitStack()
    xb = [sbuf(f"p1x{i}", [128, D], F32, p1s) for i in range(2)]
    sqb = sbuf("p1sq", [128, D], F32, p1s)
    stt = sbuf("p1st", [128, 8], F32, p1s)
    xnb = [sbuf(f"p1xn{i}", [128, D], BF16, p1s) for i in range(2)]
    xnfm_sb = [sbuf(f"p1fm{i}", [128, D], BF16, p1s) for i in range(2)]
    tpsb = [psum(f"p1tp{i}", [128, D], BF16, p1s) for i in range(2)]
    DL1, CS1, CV1, CT1, DG1, DV1, EV1 = dl.v, csc.v, cv.v, ct.v, dgq.v, dv.v, ev2.v
    NS1 = 3

    def p1_sync(q):
        wait_snap(q, SN_P0)
        for j in range(32):
            if j >= 2:
                q.wait_ge(csc.h, CS1 + NS1 * (j - 1))
            dl.inc(q.dma_start(out=xb[j % 2][:],
                               in_=x_tok[j * 128:(j + 1) * 128, :]), 16)
    blk.sync(p1_sync)

    def p1_sca(s):
        for j in range(32):
            s.wait_ge(dl.h, DL1 + 16 * (j + 1))
            if j >= 1:
                s.wait_ge(cv.h, CV1 + 2 * (j - 1) + 1)
            csc.inc(s.activation(sqb[:], xb[j % 2][:], AF.Square))
            s.wait_ge(cv.h, CV1 + 2 * j + 1)
            schain(s, s.activation(stt[:, 5:6], stt[:, 4:5], AF.Ln, bias=EPS))
            csc.inc(s.activation(stt[:, 6:7], stt[:, 5:6], AF.Exp, scale=-0.5))
            s.wait_ge(cv.h, CV1 + 2 * j + 2)
            if j >= 2:
                s.wait_ge(ct.h, CT1 + (j - 1))
                s.wait_ge(dgq.h, DG1 + 16 * (j - 1) + 256 * (j - 2))
            csc.inc(s.activation(xnb[j % 2][:], xb[j % 2][:], AF.Identity,
                                 scale=stt[:, 6:7], bias=stt[:, 7:8]))
    blk.scalar(p1_sca)

    def p1_vec(v):
        for j in range(32):
            v.wait_ge(csc.h, CS1 + NS1 * j + 1)
            vchain(v, v.reduce_sum(stt[:, 0:1], xb[j % 2][:], AX.X))
            vchain(v, v.reduce_sum(stt[:, 1:2], sqb[:], AX.X))
            vchain(v, v.tensor_scalar_mul(stt[:, 2:3], stt[:, 0:1], 1.0 / D))
            vchain(v, v.tensor_mul(stt[:, 3:4], stt[:, 2:3], stt[:, 2:3]))
            cv.inc(v.scalar_tensor_tensor(stt[:, 4:5], stt[:, 1:2], 1.0 / D,
                                          stt[:, 3:4], ALU.mult, ALU.subtract))
            v.wait_ge(csc.h, CS1 + NS1 * j + 2)
            cv.inc(v.scalar_tensor_tensor(stt[:, 7:8], stt[:, 2:3], -1.0,
                                          stt[:, 6:7], ALU.mult, ALU.mult))
            if j >= 1:
                jj = j - 1
                v.wait_ge(ct.h, CT1 + jj + 1)
                if jj >= 2:
                    v.wait_ge(dgq.h, DG1 + 16 * jj + 256 * (jj - 1))
                ev2.inc(v.tensor_copy(xnfm_sb[jj % 2][:], tpsb[jj % 2][:]))
        jj = 31
        v.wait_ge(ct.h, CT1 + 32)
        v.wait_ge(dgq.h, DG1 + 16 * jj + 256 * (jj - 1))
        ev2.inc(v.tensor_copy(xnfm_sb[jj % 2][:], tpsb[jj % 2][:]))
    blk.vector(p1_vec)

    def p1_ten(t):
        for j in range(32):
            t.wait_ge(csc.h, CS1 + NS1 * j + 3)
            if j >= 2:
                t.wait_ge(ev2.h, EV1 + (j - 1))
            last = None
            for kcc in range(16):
                last = t.transpose(tpsb[j % 2][:, kcc * 128:(kcc + 1) * 128],
                                   xnb[j % 2][:, kcc * 128:(kcc + 1) * 128],
                                   c_identb[:])
            ct.inc(last)
    blk.tensor(p1_ten)

    EV1B = EV1

    def p1_gp(g):
        for j in range(32):
            g.wait_ge(csc.h, CS1 + NS1 * j + 3)
            dgq.inc(g.dma_start(out=xn_tok_bf[j * 128:(j + 1) * 128, :],
                                in_=xnb[j % 2][:]), 16)
            if j >= 1:
                jj = j - 1
                g.wait_ge(ev2.h, EV1B + jj + 1)
                for kcc in range(16):
                    dgq.inc(g.dma_start(
                        out=xn_fm_bf[kcc * 128:(kcc + 1) * 128,
                                     jj * 128:(jj + 1) * 128],
                        in_=xnfm_sb[jj % 2][:, kcc * 128:(kcc + 1) * 128]), 16)
        jj = 31
        g.wait_ge(ev2.h, EV1B + 32)
        for kcc in range(16):
            dgq.inc(g.dma_start(
                out=xn_fm_bf[kcc * 128:(kcc + 1) * 128, jj * 128:(jj + 1) * 128],
                in_=xnfm_sb[jj % 2][:, kcc * 128:(kcc + 1) * 128]), 16)
    blk.gpsimd(p1_gp)
    SN_P1 = snap()
    p1s.close()
    if UPTO < 2:
        return _finish_stub(nc, st, blk, out_tok, c_zeros, SN_P1, dl, locals())

    p2s = contextlib.ExitStack()
    xnk = sbuf("p2xnk", [128, KC * 2048], BF16, p2s)
    wsb = [sbuf(f"p2w{i}", [128, KC * 128], BF16, p2s) for i in range(2)]
    pp = [psum(f"p2pp{i}", [128, 2048], F32, p2s) for i in range(2)]
    vbb = [sbuf(f"p2vb{i}", [128, 2048], BF16, p2s) for i in range(2)]
    vbf = [sbuf(f"p2vf{i}", [128, 2048], F32, p2s) for i in range(2)]
    hshr = sbuf("p2hshr", [128, 2048], F32, p2s)
    hsh = sbuf("p2hsh", [128, 2048], F32, p2s)

    items = [("win", fb) for fb in range(73)]
    for hb in range(HCN):
        items.append(("sw1", hb))
        items.append(("sw3", hb))
    items.append(("gate", 0))
    NIT = len(items)

    def item_weight_src(kind, idx):
        if kind == "win":
            return winT_bf, idx * 128, 128
        if kind == "sw1":
            return sw1t_bf, idx * 128, 128
        if kind == "sw3":
            return sw3t_bf, idx * 128, 128
        return gwt_bf, 0, NE

    def item_store(kind, idx):
        if kind == "win":
            if idx < 32:
                return ("bf", z_bf, idx * 128, 128)
            if idx < 36:
                return ("f32", xraw_f, (idx - 32) * 128, 128)
            if idx < 40:
                return ("bf", bm_bf, (idx - 36) * 128, 128)
            if idx < 72:
                return ("bf", cm_bf, (idx - 40) * 128, 128)
            return ("bf", dtl_bf, 0, 128)
        if kind == "sw1":
            return ("hsh", None, 0, 128)
        if kind == "sw3":
            return ("bf", hm_bf, idx * 128, 128)
        return ("gate", None, 0, 8)

    DL2, CT2, EV2B, DG2, CS2 = dl.v, ct.v, ev2.v, dgq.v, csc.v
    store_flags = []
    for h in range(2):
        for i, (kind, idx) in enumerate(items):
            stj = item_store(kind, idx)
            store_flags.append(1 if stj[1] is not None else 0)
    dv_cum = np.cumsum(store_flags)

    def dl_item(h, i):
        return DL2 + 16 * (h * (16 + NIT)) + 16 * 16 + 16 * (i + 1)

    def p2_sync(q):
        for h in range(2):
            if h == 0:
                wait_snap(q, SN_P1)
                q.wait_ge(dsq.h, DSQ0 + 16 * store_cnt[J_SW13])
            else:
                q.wait_ge(ct.h, CT2 + NIT)
            for kcc in range(16):
                dl.inc(q.dma_start(
                    out=xnk[:, kcc * 2048:(kcc + 1) * 2048],
                    in_=xn_fm_bf[kcc * 128:(kcc + 1) * 128,
                                 h * 2048:(h + 1) * 2048]), 16)
            for i, (kind, idx) in enumerate(items):
                gi = h * NIT + i
                if gi >= 2:
                    q.wait_ge(ct.h, CT2 + gi - 1)
                src, col0, ncol = item_weight_src(kind, idx)
                dl.inc(q.dma_start(
                    out=wsb[gi % 2][:, 0:KC * ncol].rearrange(
                        "p (kc c) -> p kc c", c=ncol),
                    in_=src[0:D, col0:col0 + ncol].rearrange(
                        "(kc p) c -> p kc c", p=128)), 16)
    blk.sync(p2_sync)

    def p2_ten(t):
        for h in range(2):
            for i, (kind, idx) in enumerate(items):
                gi = h * NIT + i
                t.wait_ge(dl.h, dl_item(h, i))
                if gi >= 2:
                    t.wait_ge(ev2.h, EV2B + gi - 1)
                src, col0, ncol = item_weight_src(kind, idx)
                po = pp[gi % 2]
                outw = po[0:NE, :] if kind == "gate" else po[:]
                last = None
                for k in range(KC):
                    for q in range(4):
                        last = t.matmul(
                            outw[:, q * 512:(q + 1) * 512],
                            wsb[gi % 2][:, k * ncol:(k + 1) * ncol],
                            xnk[:, k * 2048 + q * 512:k * 2048 + (q + 1) * 512],
                            start=(k == 0), stop=(k == KC - 1))
                ct.inc(last)
    blk.tensor(p2_ten)

    def p2_vec(v):
        for h in range(2):
            for i, (kind, idx) in enumerate(items):
                gi = h * NIT + i
                v.wait_ge(ct.h, CT2 + gi + 1)
                stj = item_store(kind, idx)
                typ, dst, row0, _ = stj
                po = pp[gi % 2]
                if typ == "bf":
                    if gi >= 2 and dv_cum[gi - 2] > 0:
                        v.wait_ge(dgq.h, DG2 + 16 * int(dv_cum[gi - 2]))
                    if kind == "sw3":
                        v.wait_ge(csc.h, CS2 + h * HCN + idx + 1)
                        ev2.inc(v.tensor_mul(vbb[gi % 2][:], hsh[:], po[:]))
                    else:
                        ev2.inc(v.tensor_copy(vbb[gi % 2][:], po[:]))
                elif typ == "f32":
                    if gi >= 2 and dv_cum[gi - 2] > 0:
                        v.wait_ge(dgq.h, DG2 + 16 * int(dv_cum[gi - 2]))
                    ev2.inc(v.tensor_copy(vbf[gi % 2][:], po[:]))
                elif typ == "hsh":
                    if idx >= 1:
                        v.wait_ge(csc.h, CS2 + h * HCN + idx)
                    ev2.inc(v.tensor_copy(hshr[:], po[:]))
                else:
                    ev2.inc(v.tensor_copy(gsc_sb[:, h * 2048:(h + 1) * 2048],
                                          po[0:NE, :]))
    blk.vector(p2_vec)

    def p2_sca(s):
        for h in range(2):
            for hb in range(HCN):
                gi_sw1 = h * NIT + 73 + 2 * hb
                s.wait_ge(ev2.h, EV2B + gi_sw1 + 1)
                csc.inc(s.activation(hsh[:], hshr[:], AF.Silu))
    blk.scalar(p2_sca)

    def p2_gp(g):
        for h in range(2):
            for i, (kind, idx) in enumerate(items):
                gi = h * NIT + i
                stj = item_store(kind, idx)
                typ, dst, row0, _ = stj
                if dst is None:
                    continue
                g.wait_ge(ev2.h, EV2B + gi + 1)
                buf = vbb[gi % 2] if typ == "bf" else vbf[gi % 2]
                dgq.inc(g.dma_start(
                    out=dst[row0:row0 + 128, h * 2048:(h + 1) * 2048],
                    in_=buf[:]), 16)
    blk.gpsimd(p2_gp)
    SN_P2 = snap()
    p2s.close()
    if UPTO < 3:
        return _finish_stub(nc, st, blk, out_tok, c_zeros, SN_P2, dl, locals())

    p3s = contextlib.ExitStack()
    sc_tm = sbuf("p3sctm", [128, NE], F32, p3s)
    bi_tm = sbuf("p3bitm", [128, NE], F32, p3s)
    tsc = psum("p3tsc", [128, NE], F32, p3s)
    e1a = sbuf("p3e1", [128, 32], F32, p3s)
    e2a = sbuf("p3e2", [128, 32], F32, p3s)
    s1a = sbuf("p3s1", [128, 32], F32, p3s)
    s2a = sbuf("p3s2", [128, 32], F32, p3s)
    tmp8 = sbuf("p3tmp", [128, NE], F32, p3s)
    msk = sbuf("p3msk", [128, NE], F32, p3s)
    col1 = sbuf("p3col", [128, 4], F32, p3s)
    trp = psum("p3trp", [32, 128], F32, p3s)
    etb = sbuf("p3etb", [32, 128], F32, p3s)
    ek8 = sbuf("p3ek8", [8, NENT], F32, p3s)
    ind8 = sbuf("p3ind8", [8, NENT], F32, p3s)
    cum8 = sbuf("p3cum8", [8, NENT], F32, p3s)
    sml = sbuf("p3sml", [8, 32], F32, p3s)
    sml17 = sbuf("p3sml17", [8, 17], F32, p3s)
    smlb = sbuf("p3smlb", [8, 4], BF16, p3s)
    teg = sbuf("p3teg", [8, NGRP], F32, p3s)
    tegb = sbuf("p3tegb", [8, NGRP], BF16, p3s)
    ps81 = psum("p3ps81", [8, NGRP + 8], F32, p3s)
    tei_f = sbuf("p3teif", [1, NGRP], F32, p3s)
    posps = psum("p3posps", [1, 2048], F32, p3s)
    posrow = sbuf("p3posrow", [1, 2048], F32, p3s)
    ptm = sbuf("p3ptm", [64, 128], F32, p3s)
    ptps = psum("p3ptps", [128, 64], F32, p3s)
    posfl = sbuf("p3posfl", [128, 64], F32, p3s)
    svtm = sbuf("p3svtm", [64, 128], F32, p3s)
    svcol = sbuf("p3svcol", [128, 64], F32, p3s)
    filli = sbuf("p3filli", [128, 128], I32, p3s)
    c_ones8f = sbuf("p3ones8f", [8, 1], F32, p3s)

    SNP2 = SN_P2
    DL3, CV3, CS3, CT3, DG3, EV3 = dl.v, cv.v, csc.v, ct.v, dgq.v, ev2.v

    def p3_sca(s):
        wait_snap(s, SNP2)
        csc.inc(s.activation(gsc_sb[:], gsc_sb[:], AF.Sigmoid))
    blk.scalar(p3_sca)
    SIG = csc.v

    def p3_ten(t):
        t.wait_ge(csc.h, SIG)
        for j in range(32):
            if j >= 1:
                t.wait_ge(cv.h, CV3 + j)
            ct.inc(t.transpose(tsc[:, 0:NE], gsc_sb[0:NE, j * 128:(j + 1) * 128],
                               c_identf[0:NE, 0:NE]))
    blk.tensor(p3_ten)

    def p3_vec(v):
        def ch(instr):
            vch.inc(instr)
            v.wait_ge(vch.h, vch.v)
        for j in range(32):
            v.wait_ge(ct.h, CT3 + j + 1)
            ch(v.tensor_copy(sc_tm[:], tsc[:, 0:NE]))
            ch(v.tensor_add(bi_tm[:], sc_tm[:], c_ebias[:]))
            ch(v.reduce_max(col1[:, 0:1], bi_tm[:], AX.X))
            ch(v.tensor_scalar(msk[:], bi_tm[:], col1[:, 0:1], None, ALU.is_ge))
            ch(v.tensor_mul(tmp8[:], msk[:], c_iotam99[:]))
            ch(v.tensor_scalar_add(tmp8[:], tmp8[:], 99.0))
            ch(v.tensor_reduce(e1a[:, j:j + 1], tmp8[:], AX.X, ALU.min))
            ch(v.tensor_scalar(msk[:], c_iota8tm[:], e1a[:, j:j + 1], None,
                               ALU.is_equal))
            ch(v.tensor_mul(tmp8[:], sc_tm[:], msk[:]))
            ch(v.reduce_sum(s1a[:, j:j + 1], tmp8[:], AX.X))
            ch(v.scalar_tensor_tensor(bi_tm[:], msk[:], -1e30, bi_tm[:],
                                      ALU.mult, ALU.add))
            ch(v.reduce_max(col1[:, 1:2], bi_tm[:], AX.X))
            ch(v.tensor_scalar(msk[:], bi_tm[:], col1[:, 1:2], None, ALU.is_ge))
            ch(v.tensor_mul(tmp8[:], msk[:], c_iotam99[:]))
            ch(v.tensor_scalar_add(tmp8[:], tmp8[:], 99.0))
            ch(v.tensor_reduce(e2a[:, j:j + 1], tmp8[:], AX.X, ALU.min))
            ch(v.tensor_scalar(msk[:], c_iota8tm[:], e2a[:, j:j + 1], None,
                               ALU.is_equal))
            ch(v.tensor_mul(tmp8[:], sc_tm[:], msk[:]))
            cv.inc(v.reduce_sum(s2a[:, j:j + 1], tmp8[:], AX.X))
    blk.vector(p3_vec)
    TOP2 = cv.v

    srcs = [(e1a, 0, 0), (e2a, 0, NENT // 2), (s1a, 1, 0), (s2a, 1, NENT // 2)]

    def p3b_ten(t):
        t.wait_ge(cv.h, TOP2)
        for w, (src, row, off) in enumerate(srcs):
            if w >= 1:
                t.wait_ge(ev2.h, EV3 + w)
            ct.inc(t.transpose(trp[:, :], src[:], c_identf[:]))
    blk.tensor(p3b_ten)

    def p3b_vec(v):
        for w, (src, row, off) in enumerate(srcs):
            v.wait_ge(ct.h, CT3 + 32 + w + 1)
            if w >= 1:
                v.wait_ge(dgq.h, DG3 + 16 * w)
            ev2.inc(v.tensor_copy(etb[:], trp[:]))
    blk.vector(p3b_vec)

    def p3b_gp(g):
        for w, (src, row, off) in enumerate(srcs):
            g.wait_ge(ev2.h, EV3 + w + 1)
            dgq.inc(g.dma_start(
                out=eksv[row:row + 1, off:off + NENT // 2].rearrange(
                    "r (p c) -> (r p) c", c=128),
                in_=etb[:]), 16)
    blk.gpsimd(p3b_gp)
    EKSV_DONE = dgq.v

    DL3b = dl.v

    def p3c_sync(q):
        q.wait_ge(dgq.h, EKSV_DONE)
        for r in range(8):
            dl.inc(q.dma_start(out=ek8[r:r + 1, :], in_=eksv[0:1, :]), 16)
    blk.sync(p3c_sync)
    EK8_DONE = dl.v

    def p3c2_sync(q):
        dl.inc(q.dma_start(out=c_ones8f[:], in_=ones8x[0:8, 0:1], allow_slow_non_contiguous=True), 16)
    blk.sync(p3c2_sync)
    ONE8F_DONE = dl.v

    def p3d_vec(v):
        def ch(instr):
            vch.inc(instr)
            v.wait_ge(vch.h, vch.v)
        v.wait_ge(dl.h, ONE8F_DONE)
        ch(v.tensor_scalar(ind8[:], ek8[:], c_iota8col[:], None, ALU.is_equal))
        ch(v.tensor_tensor_scan(cum8[:], ind8[:], ind8[:], 0.0, ALU.add, ALU.bypass))
        ch(v.tensor_copy(sml[:, 0:1], cum8[:, NENT - 1:NENT]))
        bcast = sml[:, 0:1].broadcast_to([8, 17]) if hasattr(sml[:, 0:1], "broadcast_to") else None
        ch(v.tensor_tensor(sml17[:], sml[:, 0:1].to_broadcast([8, 17]), c_thr17[:],
                           ALU.is_gt))
        ch(v.reduce_sum(sml[:, 1:2], sml17[:], AX.X))
        ch(v.tensor_scalar_mul(sml[:, 2:3], sml[:, 1:2], float(GRP)))
        cv.inc(v.tensor_copy(smlb[:, 0:1], sml[:, 2:3]))
    blk.vector(p3d_vec)
    PAD_DONE = cv.v

    def p3d_ten(t):
        t.wait_ge(cv.h, PAD_DONE)
        ct.inc(t.matmul(ps81[:, 0:1], c_lt8[:], smlb[:, 0:1], start=True, stop=True))
    blk.tensor(p3d_ten)
    STARTS_MM = ct.v

    def p3e_vec(v):
        def ch(instr):
            vch.inc(instr)
            v.wait_ge(vch.h, vch.v)
        v.wait_ge(ct.h, STARTS_MM)
        ch(v.tensor_copy(sml[:, 3:4], ps81[:, 0:1]))
        ch(v.tensor_scalar_add(sml[:, 4:5], sml[:, 3:4], -1.0))
        ch(v.tensor_tensor(teg[:], sml[:, 3:4].to_broadcast([8, NGRP]), c_gthr[:],
                           ALU.is_le))
        cv.inc(v.tensor_copy(tegb[:], teg[:]))
    blk.vector(p3e_vec)
    TEG_DONE = cv.v

    def p3e_ten(t):
        t.wait_ge(cv.h, TEG_DONE)
        ct.inc(t.matmul(ps81[0:1, 8:8 + NGRP], c_ones8b[:], tegb[:],
                        start=True, stop=True))
    blk.tensor(p3e_ten)
    TE_MM = ct.v

    def p3f_vec(v):
        def ch(instr):
            vch.inc(instr)
            v.wait_ge(vch.h, vch.v)
        v.wait_ge(ct.h, TE_MM)
        ch(v.tensor_scalar_add(tei_f[:], ps81[0:1, 8:8 + NGRP], -1.0))
        ch(v.tensor_copy(tei_i32[:], tei_f[:]))
        ch(v.tensor_scalar_mul(msk[0:1, 0:NGRP] if False else tei_f[:], tei_f[:], 1.0))
        ch(v.tensor_scalar(posfl[0:1, 0:NGRP], tei_f[:], float(D), None, ALU.mult))
        ch(v.tensor_copy(tei2048[:], posfl[0:1, 0:NGRP]))
        ch(v.tensor_scalar(posfl[0:1, 0:NGRP], tei_f[:], float(HID), None, ALU.mult))
        cv.inc(v.tensor_copy(tei1408[:], posfl[0:1, 0:NGRP]))
        ch(v.tensor_scalar(ek8[:], cum8[:], sml[:, 4:5], None, ALU.add))
        cv.inc(v.tensor_mul(ek8[:], ek8[:], ind8[:]))
    blk.vector(p3f_vec)
    VAL_DONE = cv.v

    def p3g_ten(t):
        t.wait_ge(cv.h, VAL_DONE)
        for r in range(4):
            if r >= 1:
                t.wait_ge(ev2.h, EV3 + 4 + r)
            last = None
            for q in range(4):
                last = t.matmul(posps[:, q * 512:(q + 1) * 512], c_ones8f[:],
                                ek8[:, r * 2048 + q * 512:r * 2048 + (q + 1) * 512],
                                start=True, stop=True)
            ct.inc(last)
    blk.tensor(p3g_ten)
    CT3G = ct.v

    DG3G = dgq.v

    def p3g_vec(v):
        for r in range(4):
            v.wait_ge(ct.h, CT3G - 4 + r + 1)
            if r >= 1:
                v.wait_ge(dgq.h, DG3G + 16 * r)
            ev2.inc(v.tensor_copy(posrow[0:1, :], posps[:]))
    blk.vector(p3g_vec)
    POSROW_DONE = ev2.v

    def p3h_gp(g):
        for r in range(4):
            g.wait_ge(ev2.h, POSROW_DONE - 4 + r + 1)
            dgq.inc(g.dma_start(
                out=pos_dram[r * 2048:(r + 1) * 2048].rearrange(
                    "(r c) -> r c", r=1),
                in_=posrow[0:1, :]), 16)
        dgq.inc(g.dma_start(
            out=tok_tab[0:NROWS, 0:1].rearrange("(p c) r -> p (c r)", p=128),
            in_=fill_oob[:]), 16)
        dgq.inc(g.dma_start(
            out=sco_tab[0:NROWS, 0:1].rearrange("(p c) r -> p (c r)", p=128),
            in_=c_zeros[:]), 16)
    blk.gpsimd(p3h_gp)
    POS_DRAM_DONE = dgq.v

    def p3i_sync(q):
        q.wait_ge(dgq.h, POS_DRAM_DONE)
        dl.inc(q.dma_start(out=ptm[:], in_=pos_dram[:].rearrange(
            "(t c) -> t c", c=128)), 16)
        dl.inc(q.dma_start(out=svtm[:], in_=eksv[1:2, :].rearrange(
            "r (t c) -> (r t) c", c=128)), 16)
    blk.sync(p3i_sync)
    PTM_DONE = dl.v

    def p3i_ten(t):
        t.wait_ge(dl.h, PTM_DONE)
        ct.inc(t.transpose(ptps[:, :], ptm[:], c_identf[0:64, 0:64]))
    blk.tensor(p3i_ten)
    PT1 = ct.v

    def p3i_vec(v):
        v.wait_ge(ct.h, PT1)
        ev2.inc(v.tensor_copy(posfl[:], ptps[:]))
        cv.inc(v.tensor_copy(posi[:], posfl[:]))
    blk.vector(p3i_vec)
    POSI_V = cv.v

    def p3j_ten(t):
        t.wait_ge(ev2.h, POSROW_DONE + 1)
        ct.inc(t.transpose(ptps[:, :], svtm[:], c_identf[0:64, 0:64]))
    blk.tensor(p3j_ten)
    PT2 = ct.v

    def p3j_vec(v):
        v.wait_ge(ct.h, PT2)
        cv.inc(v.tensor_copy(svcol[:], ptps[:]))
    blk.vector(p3j_vec)
    SVCOL_V = cv.v

    def p3k_gp(g):
        g.wait_ge(cv.h, POSI_V)
        g.wait_ge(cv.h, SVCOL_V)
        for c in range(64):
            dgq.inc(g.indirect_dma_start(
                out=tok_tab, out_offset=IndirectOffsetOnAxis(posi[:, c:c + 1], 0),
                in_=c_tokid[:, c:c + 1], in_offset=None), 16)
            dgq.inc(g.indirect_dma_start(
                out=sco_tab, out_offset=IndirectOffsetOnAxis(posi[:, c:c + 1], 0),
                in_=svcol[:, c:c + 1], in_offset=None), 16)
    blk.gpsimd(p3k_gp)
    SCAT_DONE = dgq.v

    def p3l_sync(q):
        q.wait_ge(dgq.h, SCAT_DONE)
        for c in range(NROWS // 128):
            dl.inc(q.dma_start(out=rowidx[:, c:c + 1],
                               in_=tok_tab[c * 128:(c + 1) * 128, 0:1]), 16)
            dl.inc(q.dma_start(out=scorow[:, c:c + 1],
                               in_=sco_tab[c * 128:(c + 1) * 128, 0:1]), 16)
    blk.sync(p3l_sync)
    ROWIDX_DONE = dl.v
    SN_P3 = snap()
    p3s.close()
    if UPTO < 4:
        return _finish_stub(nc, st, blk, out_tok, c_zeros, SN_P3, dl, locals())

    p4s = contextlib.ExitStack()
    cmb = [sbuf(f"p4cm{i}", [128, 2048], BF16, p4s) for i in range(2)]
    bmb = [sbuf(f"p4bm{i}", [16, 2048], BF16, p4s) for i in range(2)]
    xrw = sbuf("p4xr", [16, 2048], F32, p4s)
    xr_sb = sbuf("p4xrsb", [128, 2054], F32, p4s)
    zbs = sbuf("p4z", [128, 2048], BF16, p4s)
    dtlb = [sbuf(f"p4dtl{i}", [128, 2048], BF16, p4s) for i in range(2)]
    delta = sbuf("p4delta", [128, 2048], F32, p4s)
    ubuf = sbuf("p4u", [128, 2048], F32, p4s)
    dubuf = sbuf("p4du", [128, 2048], F32, p4s)
    ua = sbuf("p4ua", [128, 1024], F32, p4s)
    ub = sbuf("p4ub", [128, 1024], F32, p4s)
    tsp1 = sbuf("p4sp1", [128, 1024], F32, p4s)
    tsp2 = sbuf("p4sp2", [128, 1024], F32, p4s)
    app = [sbuf(f"p4a{i}", [128, 2048], F32, p4s) for i in range(2)]
    xone = sbuf("p4x", [128, 2048], F32, p4s)
    sone = sbuf("p4s", [128, 2048], F32, p4s)
    scone = sbuf("p4sc", [128, 2048], F32, p4s)
    ybuf = sbuf("p4y", [128, 2048], F32, p4s)
    szb = sbuf("p4sz", [128, 2048], F32, p4s)
    ymb = [sbuf(f"p4ymb{i}", [128, 2048], BF16, p4s) for i in range(2)]
    dbgxr = sbuf("p4dbgxr", [128, 2048], F32, p4s) if DBG_SCAN else None

    SNP3 = SN_P3
    DL4, CT4, CV4, CS4, DG4 = dl.v, ct.v, cv.v, csc.v, dgq.v
    T4B0, T4C0 = t4b.v, t4c.v
    NV, NS = 80, 27
    NIT4 = 2 if DBG_SCAN else 64

    def p4_sync(q):
        wait_snap(q, SNP3)
        for b in range(2):
            dl.inc(q.dma_start(out=dtlb[b][:],
                               in_=dtl_bf[0:128, b * 2048:(b + 1) * 2048]), 16)
        dl.inc(q.dma_start(out=xr_sb[:, 0:3], in_=zeros128[:, 0:3]), 16)
        for it in range(NIT4):
            b, g0 = it // 32, it % 32
            if it >= 2:
                q.wait_ge(t4c.h, T4C0 + 16 * (it - 1))
            dl.inc(q.dma_start(out=cmb[it % 2][:],
                               in_=cm_bf[g0 * 128:(g0 + 1) * 128,
                                         b * 2048:(b + 1) * 2048]), 16)
            if it >= 2:
                q.wait_ge(t4b.h, T4B0 + 16 * (it - 1))
            dl.inc(q.dma_start(out=bmb[it % 2][:],
                               in_=bm_bf[g0 * 16:(g0 + 1) * 16,
                                         b * 2048:(b + 1) * 2048]), 16)
            if it >= 1:
                q.wait_ge(ct.h, CT4 + 4 * it)
            dl.inc(q.dma_start(out=xrw[:],
                               in_=xraw_f[g0 * 16:(g0 + 1) * 16,
                                          b * 2048:(b + 1) * 2048]), 16)
            if it >= 1:
                q.wait_ge(csc.h, CS4 + NS * it)
            dl.inc(q.dma_start(out=zbs[:],
                               in_=z_bf[g0 * 128:(g0 + 1) * 128,
                                        b * 2048:(b + 1) * 2048]), 16)
    blk.sync(p4_sync)

    def p4_ld(it):
        return DL4 + 48 + 64 * (it + 1)

    def p4_ten(t):
        t.wait_ge(cv.h, WDT_READY)
        for it in range(NIT4):
            b, g0 = it // 32, it % 32
            if it >= 1:
                t.wait_ge(cv.h, CV4 + NV * it)
                t.wait_ge(csc.h, CS4 + NS * it)
            t.wait_ge(dl.h, p4_ld(it))
            with nc.psum_tensor(f"p4dt{it}", [128, 2048], F32) as dtps, \
                 nc.psum_tensor(f"p4xps{it}", [128, 1024], F32) as xrps:
                for c2 in range(2):
                    last = None
                    for q in range(2):
                        last = t.matmul(
                            dtps[:, c2 * 1024 + q * 512:c2 * 1024 + (q + 1) * 512],
                            p_wdt[:, g0 * 128:(g0 + 1) * 128],
                            dtlb[b][:, c2 * 1024 + q * 512:c2 * 1024 + (q + 1) * 512],
                            start=True, stop=True)
                    ct.inc(last)
                env4[("dtps", it)] = dtps
                for c2 in range(2):
                    if c2 == 1:
                        t.wait_ge(cv.h, CV4 + NV * it + 3)
                    last = None
                    for (n0, n1) in [(0, 512), (512, 1024)]:
                        last = t.matmul(
                            xrps[:, n0:n1], c_selX[:],
                            xrw[:, c2 * 1024 + n0:c2 * 1024 + n1],
                            start=True, stop=True)
                    ct.inc(last)
                env4[("xrps", it)] = xrps
            with nc.psum_tensor(f"p4B{it}", [128, 2048], F32) as bps, \
                 nc.psum_tensor(f"p4C{it}", [128, 2048], F32) as cps:
                env4[("bps", it)] = bps
                env4[("cps", it)] = cps
                for n in range(16):
                    if n == 0:
                        t.wait_ge(cv.h, CV4 + NV * it + 12)
                        t.wait_ge(csc.h, CS4 + NS * it + 8)
                    else:
                        t.wait_ge(cv.h, CV4 + NV * it + 14 + 4 * (n - 1))
                    last = None
                    for q in range(4):
                        last = t.matmul(bps[:, q * 512:(q + 1) * 512],
                                        c_selB[:, n * 128:(n + 1) * 128],
                                        bmb[it % 2][:, q * 512:(q + 1) * 512],
                                        start=True, stop=True)
                    t4b.inc(last)
                    if n >= 1:
                        t.wait_ge(cv.h, CV4 + NV * it + 16 + 4 * (n - 1))
                    last = None
                    for q in range(4):
                        last = t.matmul(cps[:, q * 512:(q + 1) * 512],
                                        c_selC[:, n * 128:(n + 1) * 128],
                                        cmb[it % 2][:, q * 512:(q + 1) * 512],
                                        start=True, stop=True)
                    t4c.inc(last)
    env4 = {}
    blk.tensor(p4_ten)

    def p4_sca(s):
        def ch(instr):
            csc.inc(instr)
            s.wait_ge(csc.h, csc.v)
        for it in range(NIT4):
            b, g0 = it // 32, it % 32
            dtps = env4[("dtps", it)]
            for c2 in range(2):
                s.wait_ge(ct.h, CT4 + 4 * it + c2 + 1)
                if it >= 1 and c2 == 0:
                    s.wait_ge(cv.h, CV4 + NV * (it - 1) + 2)
                if c2 == 1:
                    s.wait_ge(cv.h, CV4 + NV * it + 1)
                ch(s.activation(tsp1[0:128, 0:1024], dtps[:, c2 * 1024:(c2 + 1) * 1024],
                                AF.Abs, bias=p_b2[:, g0:g0 + 1]))
                ch(s.activation(tsp1[0:128, 0:1024], tsp1[0:128, 0:1024],
                                AF.Exp, scale=-1.0))
                ch(s.activation(tsp1[0:128, 0:1024], tsp1[0:128, 0:1024],
                                AF.Ln, bias=1.0))
                ch(s.activation(tsp2[0:128, 0:1024], dtps[:, c2 * 1024:(c2 + 1) * 1024],
                                AF.Relu, bias=p_b2[:, g0:g0 + 1]))
            for c2 in range(2):
                s.wait_ge(cv.h, CV4 + NV * it + 7 + 5 * c2)
                ch(s.activation(ubuf[:, c2 * 1024:(c2 + 1) * 1024], ub[:],
                                AF.Silu, bias=p_cb[:, g0:g0 + 1]))
            for n in range(16):
                s.wait_ge(cv.h, CV4 + NV * it + 2)
                if n >= 2:
                    s.wait_ge(cv.h, CV4 + NV * it + 15 + 4 * (n - 2))
                ch(s.activation(app[n % 2][:], delta[:], AF.Exp,
                                scale=p_A[:, g0 * 16 + n:g0 * 16 + n + 1]))
            s.wait_ge(dl.h, p4_ld(it))
            if it >= 1:
                s.wait_ge(cv.h, CV4 + NV * (it - 1) + 79)
            ch(s.activation(szb[:], zbs[:], AF.Silu))
    blk.scalar(p4_sca)

    def p4_vec(v):
        def ch(instr):
            cv.inc(instr)
            v.wait_ge(cv.h, cv.v)
        for it in range(NIT4):
            b, g0 = it // 32, it % 32
            dtps = env4[("dtps", it)]
            xrps = env4[("xrps", it)]
            bps = env4[("bps", it)]
            cps = env4[("cps", it)]
            if it >= 1:
                v.wait_ge(csc.h, CS4 + NS * (it - 1) + 26)
            for c2 in range(2):
                v.wait_ge(csc.h, CS4 + NS * it + 4 * (c2 + 1))
                ch(v.tensor_add(delta[:, c2 * 1024:(c2 + 1) * 1024],
                                tsp1[0:128, 0:1024], tsp2[0:128, 0:1024]))
            for c2 in range(2):
                v.wait_ge(ct.h, CT4 + 4 * it + 2 + c2 + 1)
                if c2 == 1:
                    v.wait_ge(csc.h, CS4 + NS * it + 9)
                ch(v.tensor_copy(xr_sb[:, 3 + c2 * 1024:3 + (c2 + 1) * 1024],
                                 xrps[:]))
                ch(v.tensor_scalar(ua[:], xr_sb[:, c2 * 1024:c2 * 1024 + 1024],
                                   p_cw[:, g0 * 4:g0 * 4 + 1], None, ALU.mult))
                ch(v.scalar_tensor_tensor(ub[:], xr_sb[:, c2 * 1024 + 1:c2 * 1024 + 1025],
                                          p_cw[:, g0 * 4 + 1:g0 * 4 + 2], ua[:],
                                          ALU.mult, ALU.add))
                ch(v.scalar_tensor_tensor(ua[:], xr_sb[:, c2 * 1024 + 2:c2 * 1024 + 1026],
                                          p_cw[:, g0 * 4 + 2:g0 * 4 + 3], ub[:],
                                          ALU.mult, ALU.add))
                ch(v.scalar_tensor_tensor(ub[:], xr_sb[:, c2 * 1024 + 3:c2 * 1024 + 1027],
                                          p_cw[:, g0 * 4 + 3:g0 * 4 + 4], ua[:],
                                          ALU.mult, ALU.add))
            if DBG_SCAN and it == NIT4 - 1:
                vch.inc(v.tensor_copy(dbgxr[:, 0:2048], xr_sb[:, 3:2051]))
                v.wait_ge(vch.h, vch.v)
            v.wait_ge(csc.h, CS4 + NS * it + 10)
            ch(v.tensor_mul(dubuf[:], delta[:], ubuf[:]))
            for n in range(16):
                v.wait_ge(t4b.h, T4B0 + 16 * it + n + 1)
                ch(v.tensor_mul(xone[:], dubuf[:], bps[:]))
                v.wait_ge(csc.h, CS4 + NS * it + 10 + n + 1)
                ch(v.tensor_tensor_scan(sone[:], app[n % 2][:], xone[:], 0.0,
                                        ALU.mult, ALU.add))
                v.wait_ge(t4c.h, T4C0 + 16 * it + n + 1)
                ch(v.tensor_mul(scone[:], sone[:], cps[:]))
                if n == 0:
                    ch(v.tensor_copy(ybuf[:], scone[:]))
                else:
                    ch(v.tensor_add(ybuf[:], ybuf[:], scone[:]))
            ch(v.scalar_tensor_tensor(xone[:], ubuf[:], p_D[:, g0:g0 + 1], ybuf[:],
                                      ALU.mult, ALU.add))
            v.wait_ge(csc.h, CS4 + NS * it + 27)
            ch(v.tensor_mul(scone[:], xone[:], szb[:]))
            if it >= 2:
                v.wait_ge(dgq.h, DG4 + 16 * (it - 1))
            ch(v.tensor_copy(ymb[it % 2][:], scone[:]))
    blk.vector(p4_vec)

    def p4_gp(g):
        for it in range(NIT4):
            b, g0 = it // 32, it % 32
            g.wait_ge(cv.h, CV4 + NV * it + 80)
            dgq.inc(g.dma_start(out=ym_bf[g0 * 128:(g0 + 1) * 128,
                                          b * 2048:(b + 1) * 2048],
                                in_=ymb[it % 2][:]), 16)
        if DBG_SCAN:
            g.wait_ge(cv.h, CV4 + NV * NIT4)
            g.wait_ge(csc.h, CS4 + NS * NIT4)
            for r, buf in enumerate([delta, ubuf, dubuf, dbgxr, sone, ybuf]):
                dgq.inc(g.dma_start(out=dbg_scan[r * 128:(r + 1) * 128, :],
                                    in_=buf[:]), 16)
    blk.gpsimd(p4_gp)
    YM_DONE = dgq.v
    SN_P4 = snap()
    p4s.close()
    if UPTO < 5:
        return _finish_stub(nc, st, blk, out_tok, c_zeros, SN_P4, dl, locals())

    from bass_rust import RuntimeValue
    p5s = contextlib.ExitStack()
    rint = [sbuf(f"p5ri{i}", [128, D], BF16, p5s) for i in range(2)]
    rinT = sbuf("p5rT", [128, KC * GRP], BF16, p5s)
    w1b = [sbuf(f"p5w1{i}", [128, KC * 256], BF16, p5s) for i in range(2)]
    w3b = [sbuf(f"p5w3{i}", [128, KC * 256], BF16, p5s) for i in range(2)]
    w2b = [sbuf(f"p5w2{i}", [128, HCN * 512], BF16, p5s) for i in range(2)]
    hmsb = sbuf("p5hm", [128, HCN * GRP], BF16, p5s)
    hsb5 = sbuf("p5hs", [128, GRP], F32, p5s)
    dnb = [sbuf(f"p5dn{i}", [128, 512], BF16, p5s) for i in range(2)]

    SNP4 = SN_P4
    CT5, CV5, CS5, DG5, DS5, T4B5 = ct.v, cv.v, csc.v, dgq.v, dsq.v, t4b.v
    DL5 = dl.v
    DG5_adj = [DG5]
    HQS = [(0, 2), (2, 2), (4, 2), (6, 2), (8, 2), (10, 1)]
    NGRP_RUN = NGRP_RUN_OVERRIDE if NGRP_RUN_OVERRIDE else NGRP
    NT5 = 8 + 96
    NV5 = 59
    NS5 = HCN
    env5 = {}

    def ct5(g):
        return CT5 + NT5 * g

    def cv5(g):
        return CV5 + NV5 * g

    def lbl_scale(r):
        return 1 if r == 0 else 2 * r

    def lbl_evac(r):
        return 2 * r + 3 if r <= 6 else 16

    def p5_gp(g_):
        wait_snap(g_, SNP4)
        g_.wait_ge(dsq.h, ROUTED_READY)
        g_.wait_ge(dl.h, ROWIDX_DONE)
        dgq.inc(g_.dma_start(out=dbg_tei, in_=tei_i32[0:1, :]), 16)
        for g in range(NGRP_RUN):
            if ST5 >= 3:
                if g >= 2:
                    g_.wait_ge(dl.h, DL5 + 16 * 16 * (g - 1))
                r1 = g_.alloc_register()
                g_.reg_load(r1, tei2048[0:1, g:g + 1])
                rv1 = RuntimeValue(val=r1, min_val=0, max_val=7 * D)
                dgq.inc(g_.dma_start(out=w1stg[g % 2], in_=w1t_bf[ds(rv1, D), :]), 16)
                dgq.inc(g_.dma_start(out=w3stg[g % 2], in_=w3t_bf[ds(rv1, D), :]), 16)
                g_.free_register(r1)
            else:
                dgq.inc(g_.dma_start(out=dbg_tei, in_=tei_i32[0:1, :]), 16)
                dgq.inc(g_.dma_start(out=dbg_tei, in_=tei_i32[0:1, :]), 16)
            for rsub in range(8):
                gb = g * 8 + rsub
                if gb >= 2 and ST5 >= 2:
                    pj = gb - 2
                    g_.wait_ge(ct.h, ct5(pj // 8) + (pj % 8) + 1)
                dgq.inc(g_.indirect_dma_start(
                    out=rint[rsub % 2][:], out_offset=None,
                    in_=xn_tok_bf,
                    in_offset=IndirectOffsetOnAxis(
                        rowidx[:, gb:gb + 1], 0)), 16)
            if ST5 < 5:
                continue
            for dc in range(4):
                for rsub in range(8):
                    g_.wait_ge(cv.h, cv5(g) + 27 + dc * 8 + rsub + 1)
                    dgq.inc(g_.dma_start(
                        out=so_bf[g * GRP + rsub * 128:g * GRP + (rsub + 1) * 128,
                                  dc * 512:(dc + 1) * 512],
                        in_=dnb[(dc * 8 + rsub) % 2][:]), 16)
    blk.gpsimd(p5_gp)

    def p5_ten(t):
        for g in range(NGRP_RUN):
            if g >= 1:
                t.wait_ge(cv.h, cv5(g))
            if ST5 < 2 and ST5 != 12:
                continue
            with nc.psum_tensor(f"p5tp{g}", [128, 2048], BF16) as tp5:
                for rsub in range(8):
                    t.wait_ge(cv.h, cv5(g) + lbl_scale(rsub))
                    if rsub >= 1:
                        t.wait_ge(cv.h, cv5(g) + lbl_evac(rsub - 1))
                    env5[(g, "tp5")] = tp5
                    last = None
                    for kc in range(KC):
                        last = t.transpose(tp5[:, kc * 128:(kc + 1) * 128],
                                           rint[rsub % 2][:, kc * 128:(kc + 1) * 128],
                                           c_identb[:])
                    ct.inc(last)
            if ST5 < 3:
                continue
            with nc.psum_tensor(f"p5h1a{g}", [128, GRP], F32) as h1a, \
                 nc.psum_tensor(f"p5h1b{g}", [128, GRP], F32) as h1b, \
                 nc.psum_tensor(f"p5h3a{g}", [128, GRP], F32) as h3a, \
                 nc.psum_tensor(f"p5h3b{g}", [128, GRP], F32) as h3b:
                for hb in range(HCN):
                    env5[(g, "h1", hb)] = h1a if hb % 2 == 0 else h1b
                    env5[(g, "h3", hb)] = h3a if hb % 2 == 0 else h3b
                for j in range(96):
                    ihq, kc = j // 16, j % 16
                    h0, hqn = HQS[ihq]
                    t.wait_ge(dl.h, DL5 + 16 * (16 * g + 2 * ihq + 2))
                    if j == 0:
                        t.wait_ge(cv.h, cv5(g) + 16)
                    if kc == 0 and ihq >= 1:
                        t.wait_ge(cv.h, cv5(g) + 16 + h0)
                    last = None
                    for hbl in range(hqn):
                        hb = h0 + hbl
                        for (wb, pse) in ((w1b, env5[(g, "h1", hb)]),
                                          (w3b, env5[(g, "h3", hb)])):
                            for rh in range(2):
                                last = t.matmul(
                                    pse[:, rh * 512:(rh + 1) * 512],
                                    wb[ihq % 2][:, kc * hqn * 128 + hbl * 128:
                                                kc * hqn * 128 + (hbl + 1) * 128],
                                    rinT[:, kc * GRP + rh * 512:kc * GRP + (rh + 1) * 512],
                                    start=(kc == 0), stop=(kc == KC - 1))
                    ct.inc(last)
            if ST5 < 5:
                continue
            with contextlib.ExitStack() as dstk:
                dnps = [dstk.enter_context(
                    nc.psum_tensor(f"p5d{g}_{r}", [128, 512], F32))
                    for r in range(8)]
                for r in range(8):
                    env5[(g, "dn", r)] = dnps[r]
                t.wait_ge(cv.h, cv5(g) + 27)
                for dc in range(4):
                    t.wait_ge(dl.h, DL5 + 16 * (16 * g + 12 + dc + 1))
                    for hc in range(HCN):
                        last = None
                        for rsub in range(8):
                            if hc == 0 and dc >= 1:
                                t.wait_ge(cv.h, cv5(g) + 27 + 8 * (dc - 1) + rsub + 1)
                            last = t.matmul(
                                dnps[rsub][:],
                                hmsb[:, hc * GRP + rsub * 128:hc * GRP + (rsub + 1) * 128],
                                w2b[dc % 2][:, hc * 512:(hc + 1) * 512],
                                start=(hc == 0), stop=(hc == HCN - 1))
                        t4b.inc(last)
    blk.tensor(p5_ten)

    def p5_sca(s):
        s.wait_ge(dsq.h, ROUTED_READY)
        s.wait_ge(dl.h, ROWIDX_DONE)
        for g in range(NGRP_RUN):
            if ST5 >= 5:
                if g >= 2:
                    s.wait_ge(dl.h, DL5 + 16 * 16 * (g - 1))
                r2 = s.alloc_register()
                s.reg_load(r2, tei1408[0:1, g:g + 1])
                rv2 = RuntimeValue(val=r2, min_val=0, max_val=7 * HID)
                dsq.inc(s.dma_start(out=w2stg[g % 2], in_=w2t_bf[ds(rv2, HID), :]), 16)
                s.free_register(r2)
            if ST5 < 4:
                continue
            for hb in range(HCN):
                ihq = next(i for i, (h0_, n_) in enumerate(HQS)
                           if h0_ <= hb < h0_ + n_)
                s.wait_ge(ct.h, ct5(g) + 8 + (ihq + 1) * 16)
                if hb >= 1:
                    s.wait_ge(cv.h, cv5(g) + 16 + hb)
                csc.inc(s.activation(hsb5[:], env5[(g, "h1", hb)][:], AF.Silu))
                s.wait_ge(csc.h, csc.v)
    blk.scalar(p5_sca)

    def p5_sync(q):
        for g in range(NGRP_RUN):
            if ST5 < 3:
                continue
            q.wait_ge(dgq.h, DG5 + 16 * (1 + 42 * g + 2))
            for ihq, (h0, hqn) in enumerate(HQS):
                if ihq >= 2:
                    q.wait_ge(ct.h, ct5(g) + 8 + (ihq - 1) * 16)
                elif g >= 1:
                    q.wait_ge(ct.h, ct5(g))
                dl.inc(q.dma_start(
                    out=w1b[ihq % 2][:, 0:KC * hqn * 128].rearrange(
                        "p (kc c) -> p kc c", c=hqn * 128),
                    in_=w1stg[g % 2][:, h0 * 128:(h0 + hqn) * 128].rearrange(
                        "(kc p) c -> p kc c", p=128)), 16)
                dl.inc(q.dma_start(
                    out=w3b[ihq % 2][:, 0:KC * hqn * 128].rearrange(
                        "p (kc c) -> p kc c", c=hqn * 128),
                    in_=w3stg[g % 2][:, h0 * 128:(h0 + hqn) * 128].rearrange(
                        "(kc p) c -> p kc c", p=128)), 16)
            if ST5 < 5:
                continue
            q.wait_ge(dsq.h, DS5 + 16 * (g + 1))
            for dc in range(4):
                if dc >= 2:
                    q.wait_ge(t4b.h, T4B5 + 44 * g + (dc - 1) * 11)
                elif g >= 1:
                    q.wait_ge(t4b.h, T4B5 + 44 * g)
                dl.inc(q.dma_start(
                    out=w2b[dc % 2][:, 0:HCN * 512].rearrange(
                        "p (hc c) -> p hc c", c=512),
                    in_=w2stg[g % 2][:, dc * 512:(dc + 1) * 512].rearrange(
                        "(hc p) c -> p hc c", p=128)), 16)
    blk.sync(p5_sync)

    def p5_vec(v):
        def ch(instr):
            cv.inc(instr)
            v.wait_ge(cv.h, cv.v)
        for g in range(NGRP_RUN):
            if ST5 < 1:
                continue

            def scale_op(r):
                gb = g * 8 + r
                v.wait_ge(dgq.h, DG5 + 16 * (1 + (42 if ST5 >= 5 else 10) * g + 2 + r + 1))
                ch(v.tensor_scalar(rint[r % 2][:], rint[r % 2][:],
                                   scorow[:, gb:gb + 1], None, ALU.mult))

            def evac_op(r):
                v.wait_ge(ct.h, ct5(g) + r + 1)
                ch(v.tensor_copy(
                    rinT[:].rearrange("p (kc r) -> p kc r", r=GRP)
                        [:, :, r * 128:(r + 1) * 128],
                    env5[(g, "tp5")][:].rearrange("p (kc r) -> p kc r", r=128)))

            if ST5 < 2:
                for r in range(8):
                    scale_op(r)
                continue
            scale_op(0)
            scale_op(1)
            for r in range(2, 8):
                evac_op(r - 2)
                scale_op(r)
            evac_op(6)
            evac_op(7)
            if ST5 < 4:
                continue
            for hb in range(HCN):
                v.wait_ge(csc.h, CS5 + NS5 * g + hb + 1)
                ch(v.tensor_mul(
                    hmsb[:, hb * GRP:(hb + 1) * GRP], hsb5[:],
                    env5[(g, "h3", hb)][:]))
            if ST5 < 5:
                continue
            for dc in range(4):
                for rsub in range(8):
                    v.wait_ge(t4b.h, T4B5 + 44 * g + dc * 11 + 11)
                    si = 8 * dc + rsub
                    if si >= 2:
                        v.wait_ge(dgq.h, DG5 + 16 * (1 + 42 * g + 10 + si - 1))
                    elif g >= 1:
                        v.wait_ge(dgq.h, DG5 + 16 * (1 + 42 * g))
                    ch(v.tensor_copy(dnb[(dc * 8 + rsub) % 2][:],
                                     env5[(g, "dn", rsub)][:]))
    blk.vector(p5_vec)
    SN_P5 = snap()
    p5s.close()
    if UPTO < 6:
        return _finish_stub(nc, st, blk, out_tok, c_zeros, SN_P5, dl, locals())

    p6s = contextlib.ExitStack()
    ymt = sbuf("p6ym", [128, 32 * 1024], BF16, p6s)
    wo = [sbuf(f"p6wo{i}", [128, 32 * 128], BF16, p6s) for i in range(2)]
    stg = sbuf("p6stg", [128, 1024], F32, p6s)
    stg2 = sbuf("p6stg2", [128, 1024], F32, p6s)
    SNP5 = SN_P5
    CT6, CV6, DS6, DG6, DL6 = ct.v, cv.v, dsq.v, dgq.v, dl.v

    def p6a_sync(q):
        wait_snap(q, SNP5)
        q.wait_ge(dsq.h, ALLW_READY)
        for tc in range(4):
            if tc >= 1:
                q.wait_ge(ct.h, CT6 + 32 * tc)
            dl.inc(q.dma_start(
                out=ymt[:].rearrange("p (cc t) -> p cc t", t=1024),
                in_=ym_bf[:, tc * 1024:(tc + 1) * 1024].rearrange(
                    "(cc p) t -> p cc t", p=128)), 16)
    blk.sync(p6a_sync)

    def p6a_sca(s):
        wait_snap(s, SNP5)
        s.wait_ge(dsq.h, ALLW_READY)
        for tc in range(4):
            for db in range(16):
                k = tc * 16 + db
                if k >= 2:
                    s.wait_ge(ct.h, CT6 + 2 * (k - 1))
                dsq.inc(s.dma_start(
                    out=wo[db % 2][:].rearrange("p (cc d) -> p cc d", d=128),
                    in_=woutT_bf[:, db * 128:(db + 1) * 128].rearrange(
                        "(cc p) d -> p cc d", p=128)), 16)
    blk.scalar(p6a_sca)

    env6 = {}

    def p6a_ten(t):
        for tc in range(4):
            t.wait_ge(dl.h, DL6 + 16 * (tc + 1))
            for db in range(16):
                k = tc * 16 + db
                t.wait_ge(dsq.h, DS6 + 16 * (k + 1))
                with nc.psum_tensor(f"p6s{k}", [128, 1024], F32) as sp6, \
                     nc.psum_tensor(f"p6t{k}", [128, 1024], F32) as stps:
                    env6[(k, "sp6")] = sp6
                    env6[(k, "stps")] = stps
                    if k >= 1:
                        t.wait_ge(cv.h, CV6 + 2 * k)
                    last = None
                    for cc in range(32):
                        for rh in range(2):
                            last = t.matmul(
                                sp6[:, rh * 512:(rh + 1) * 512],
                                wo[db % 2][:, cc * 128:(cc + 1) * 128],
                                ymt[:, cc * 1024 + rh * 512:cc * 1024 + (rh + 1) * 512],
                                start=(cc == 0), stop=(cc == 31))
                    ct.inc(last)
                    t.wait_ge(cv.h, CV6 + 2 * k + 1)
                    last = None
                    for i8 in range(8):
                        last = t.transpose(stps[:, i8 * 128:(i8 + 1) * 128],
                                           stg[:, i8 * 128:(i8 + 1) * 128],
                                           c_identf[:])
                    ct.inc(last)
    blk.tensor(p6a_ten)

    def p6a_vec(v):
        for tc in range(4):
            for db in range(16):
                k = tc * 16 + db
                v.wait_ge(ct.h, CT6 + 2 * k + 1)
                cv.inc(v.tensor_copy(stg[:], env6[(k, "sp6")][:]))
                v.wait_ge(cv.h, cv.v)
                v.wait_ge(ct.h, CT6 + 2 * k + 2)
                if k >= 1:
                    v.wait_ge(dgq.h, DG6 + 16 * 8 * k)
                cv.inc(v.tensor_copy(stg2[:], env6[(k, "stps")][:]))
                v.wait_ge(cv.h, cv.v)
    blk.vector(p6a_vec)

    def p6a_gp(g_):
        for tc in range(4):
            for db in range(16):
                k = tc * 16 + db
                g_.wait_ge(cv.h, CV6 + 2 * k + 2)
                for i8 in range(8):
                    dgq.inc(g_.dma_start(
                        out=s_tm_f[tc * 1024 + i8 * 128:tc * 1024 + (i8 + 1) * 128,
                                   db * 128:(db + 1) * 128],
                        in_=stg2[:, i8 * 128:(i8 + 1) * 128]), 16)
    blk.gpsimd(p6a_gp)
    SN_P6A = snap()
    p6s.close()

    p6bs = contextlib.ExitStack()
    sw2sb = sbuf("p6sw2", [128, HCN * 2048], BF16, p6bs)
    hmt = [sbuf(f"p6hmt{i}", [128, HCN * 128], BF16, p6bs) for i in range(2)]
    stm = [sbuf(f"p6stm{i}", [128, D], F32, p6bs) for i in range(2)]
    xro = [sbuf(f"p6x{i}", [128, D], F32, p6bs) for i in range(2)]
    g0t = [sbuf(f"p6g0{i}", [128, D], BF16, p6bs) for i in range(2)]
    g1t = [sbuf(f"p6g1{i}", [128, D], BF16, p6bs) for i in range(2)]
    f0b = sbuf("p6f0", [128, D], F32, p6bs)
    f1b = sbuf("p6f1", [128, D], F32, p6bs)
    gbuf = sbuf("p6g", [128, D], F32, p6bs)
    obuf = [sbuf(f"p6o{i}", [128, D], F32, p6bs) for i in range(2)]
    shps = [psum(f"p6sh{i}", [128, D], F32, p6bs) for i in range(2)]
    SNP6A = SN_P6A
    CT6B, CV6B, CS6B, DL6B, DG6B = ct.v, cv.v, csc.v, dl.v, dgq.v
    NV6B, NS6B = 5, 2

    def p6b_sync(q):
        wait_snap(q, SNP6A)
        for hc in range(HCN):
            dl.inc(q.dma_start(
                out=sw2sb[:, hc * 2048:(hc + 1) * 2048],
                in_=sw2t_bf[hc * 128:(hc + 1) * 128, :]), 16)
        for j in range(32):
            if j >= 2:
                q.wait_ge(cv.h, CV6B + NV6B * (j - 1))
            dl.inc(q.dma_start(
                out=hmt[j % 2][:].rearrange("p (hc t) -> p hc t", t=128),
                in_=hm_bf[:, j * 128:(j + 1) * 128].rearrange(
                    "(hc p) t -> p hc t", p=128)), 16)
            dl.inc(q.dma_start(out=stm[j % 2][:],
                               in_=s_tm_f[j * 128:(j + 1) * 128, :]), 16)
            dl.inc(q.dma_start(out=xro[j % 2][:],
                               in_=x_tok[j * 128:(j + 1) * 128, :]), 16)
    blk.sync(p6b_sync)

    def p6b_gp(g_):
        wait_snap(g_, SNP6A)
        for j in range(32):
            if j >= 2:
                g_.wait_ge(cv.h, CV6B + NV6B * (j - 1) + 1)
            dgq.inc(g_.indirect_dma_start(
                out=g0t[j % 2][:], out_offset=None, in_=so_bf,
                in_offset=IndirectOffsetOnAxis(posi[:, j:j + 1], 0)), 16)
            dgq.inc(g_.indirect_dma_start(
                out=g1t[j % 2][:], out_offset=None, in_=so_bf,
                in_offset=IndirectOffsetOnAxis(posi[:, 32 + j:32 + j + 1], 0)), 16)
            g_.wait_ge(cv.h, CV6B + NV6B * j + 5)
            dgq.inc(g_.dma_start(out=out_tok[j * 128:(j + 1) * 128, :],
                                 in_=obuf[j % 2][:]), 16)
        g_.wait_ge(dgq.h, dgq.v)
    blk.gpsimd(p6b_gp)

    def p6b_ten(t):
        for j in range(32):
            t.wait_ge(dl.h, DL6B + 16 * HCN + 16 * 3 * (j + 1))
            if j >= 2:
                t.wait_ge(cv.h, CV6B + NV6B * (j - 1) + 3)
            last = None
            for hc in range(HCN):
                for rh in range(4):
                    last = t.matmul(
                        shps[j % 2][:, rh * 512:(rh + 1) * 512],
                        hmt[j % 2][:, hc * 128:(hc + 1) * 128],
                        sw2sb[:, hc * 2048 + rh * 512:hc * 2048 + (rh + 1) * 512],
                        start=(hc == 0), stop=(hc == HCN - 1))
            ct.inc(last)
    blk.tensor(p6b_ten)

    def p6b_sca(s):
        for j in range(32):
            s.wait_ge(dgq.h, DG6B + 16 * (3 * j + 2))
            if j >= 1:
                s.wait_ge(cv.h, CV6B + NV6B * (j - 1) + 2)
                s.wait_ge(cv.h, CV6B + NV6B * (j - 1) + 4)
            csc.inc(s.activation(f1b[:], g1t[j % 2][:], AF.Copy))
            s.wait_ge(csc.h, csc.v)
            s.wait_ge(cv.h, CV6B + NV6B * j + 3)
            csc.inc(s.activation(gbuf[:], f0b[:], AF.Silu))
            s.wait_ge(csc.h, csc.v)
    blk.scalar(p6b_sca)

    def p6b_vec(v):
        def ch(instr):
            cv.inc(instr)
            v.wait_ge(cv.h, cv.v)
        for j in range(32):
            v.wait_ge(dgq.h, DG6B + 16 * (3 * j + 1))
            if j >= 1:
                v.wait_ge(csc.h, CS6B + NS6B * (j - 1) + 2)
            ch(v.tensor_copy(f0b[:], g0t[j % 2][:]))
            v.wait_ge(csc.h, CS6B + NS6B * j + 1)
            ch(v.tensor_add(f0b[:], f0b[:], f1b[:]))
            v.wait_ge(ct.h, CT6B + j + 1)
            ch(v.tensor_add(f0b[:], f0b[:], shps[j % 2][:]))
            v.wait_ge(csc.h, CS6B + NS6B * j + 2)
            v.wait_ge(dl.h, DL6B + 16 * HCN + 16 * 3 * (j + 1))
            if j >= 2:
                v.wait_ge(dgq.h, DG6B + 16 * (3 * (j - 1)))
            ch(v.tensor_mul(obuf[j % 2][:], stm[j % 2][:], gbuf[:]))
            ch(v.tensor_add(obuf[j % 2][:], obuf[j % 2][:], xro[j % 2][:]))
    blk.vector(p6b_vec)
    p6bs.close()
    return nc, st, locals()


def _finish_stub(nc, st, blk, out_tok, c_zeros, sn, dl, env):
    fin = Cnt(st.enter_context(nc.semaphore("fin")))

    def fs(q):
        for c, v in sn.items():
            if v > 0:
                q.wait_ge(c.h, v)
        for j in range(NTOK // 128):
            for cc in range(D // 128):
                fin.inc(q.dma_start(out=out_tok[j * 128:(j + 1) * 128,
                                                cc * 128:(cc + 1) * 128],
                                    in_=c_zeros[:]), 16)
        q.wait_ge(fin.h, fin.v)
    blk.gpsimd(fs)
    return nc, st, env


def _host_prep(inputs):
    ap = {}
    hs = np.asarray(inputs["L0_attn_hidden_states"])
    ap["x_tok"] = hs.reshape(B_GLOBAL, L, D)
    ap["W_inT"] = np.ascontiguousarray(np.asarray(inputs["L0_attn_W_in"]).T)
    ap["W_dtT"] = np.ascontiguousarray(np.asarray(inputs["L0_attn_W_dt"]).T)
    ap["b_dt_l"] = np.ascontiguousarray(
        np.asarray(inputs["L0_attn_b_dt"]).reshape(NBLK, 128).T)
    cw = np.asarray(inputs["L0_attn_conv_w"])[:, 0, :]
    ap["conv_w_l"] = np.ascontiguousarray(
        cw.reshape(NBLK, 128, D_CONV).transpose(1, 0, 2).reshape(128, NBLK * D_CONV))
    ap["conv_b_l"] = np.ascontiguousarray(
        np.asarray(inputs["L0_attn_conv_b"]).reshape(NBLK, 128).T)
    ap["A_log_l"] = np.ascontiguousarray(
        np.asarray(inputs["L0_attn_A_log"]).reshape(NBLK, 128, D_STATE)
        .transpose(1, 0, 2).reshape(128, NBLK * D_STATE))
    ap["D_l"] = np.ascontiguousarray(
        np.asarray(inputs["L0_attn_D_param"]).reshape(NBLK, 128).T)
    ap["W_outT"] = np.ascontiguousarray(np.asarray(inputs["L0_attn_W_out"]).T)
    ap["gate_wT"] = np.ascontiguousarray(np.asarray(inputs["L0_ffn_gate_w"]).T)
    ap["w1T"] = np.ascontiguousarray(
        np.asarray(inputs["L0_ffn_w1"]).transpose(0, 2, 1).reshape(NE * D, HID))
    ap["w3T"] = np.ascontiguousarray(
        np.asarray(inputs["L0_ffn_w3"]).transpose(0, 2, 1).reshape(NE * D, HID))
    ap["w2T"] = np.ascontiguousarray(
        np.asarray(inputs["L0_ffn_w2"]).transpose(0, 2, 1).reshape(NE * HID, D))
    ap["sw1T"] = np.ascontiguousarray(np.asarray(inputs["L0_ffn_sw1"]).T)
    ap["sw3T"] = np.ascontiguousarray(np.asarray(inputs["L0_ffn_sw3"]).T)
    ap["sw2T"] = np.ascontiguousarray(np.asarray(inputs["L0_ffn_sw2"]).T)
    eb = np.asarray(inputs["L0_ffn_expert_bias"]).astype(np.float32)
    ap["ebias_tm"] = np.ascontiguousarray(np.broadcast_to(eb[None, :], (128, NE)))

    ap["identb"] = np.eye(128, dtype=BF)
    ap["identf"] = np.eye(128, dtype=np.float32)
    selB = np.zeros((16, 16 * 128), dtype=BF)
    for n in range(16):
        selB[n, n * 128:(n + 1) * 128] = BF(1.0)
    ap["selB"] = selB
    selC = np.zeros((128, 16 * 128), dtype=BF)
    for n in range(16):
        for p in range(128):
            selC[(p // 16) * 16 + n, n * 128 + p] = BF(1.0)
    ap["selC_in"] = selC
    selX = np.zeros((16, 128), dtype=np.float32)
    for p in range(128):
        selX[p % 16, p] = 1.0
    ap["selX"] = selX
    ap["ones8b"] = np.ones((8, 1), dtype=BF)
    lt8 = np.zeros((8, 8), dtype=BF)
    for q in range(8):
        for p in range(8):
            if q < p:
                lt8[q, p] = BF(1.0)
    ap["lt8"] = lt8
    ap["ones1x128"] = np.ones((1, 128), dtype=BF)
    ap["iota8col"] = np.arange(8, dtype=np.float32).reshape(8, 1)
    ap["iota8tm"] = np.broadcast_to(
        np.arange(8, dtype=np.float32)[None, :], (128, NE)).copy()
    ap["big_tm"] = np.full((128, NE), 99.0, dtype=np.float32)
    ap["iotam99"] = (np.broadcast_to(np.arange(8, dtype=np.float32)[None, :],
                                     (128, NE)) - 99.0).copy()
    ap["neginf_tm"] = np.full((128, NE), -1e30, dtype=np.float32)
    ap["thr17"] = np.broadcast_to(
        (np.arange(17, dtype=np.float32) * GRP)[None, :], (8, 17)).copy()
    ap["gthr"] = np.broadcast_to(
        (np.arange(NGRP, dtype=np.float32) * GRP)[None, :], (8, NGRP)).copy()
    tokid = np.zeros((128, 64), dtype=np.int32)
    for t in range(64):
        tokid[:, t] = (t % 32) * 128 + np.arange(128)
    ap["tokid_i"] = tokid
    ap["fill_oob"] = np.full((128, 128), NTOK, dtype=np.int32)
    ap["zeros128"] = np.zeros((128, 128), dtype=np.float32)
    ap["ones8x"] = np.ones((8, NENT), dtype=np.float32)
    ap["zeros_bf_row"] = np.zeros((1, D), dtype=BF)
    cval = np.zeros((128, 4), dtype=np.float32)
    cval[:, 0] = EPS
    cval[:, 1] = -0.5
    cval[:, 2] = -1.0
    ap["constvals"] = cval
    return ap


def kernel(**inputs):
    ap = _host_prep(inputs)
    nc, st, env = build_nc()
    xfull = ap.pop("x_tok")
    in_maps = []
    for c in range(NCORE):
        m = dict(ap)
        m["x_tok"] = np.ascontiguousarray(
            xfull[2 * c:2 * c + 2].reshape(NTOK, D))
        in_maps.append(m)
    res = run_bass_kernel_spmd(nc, in_maps, core_ids=list(range(NCORE)))
    st.close()
    outs = [r["out_tok"] for r in res.results]
    return np.concatenate([o.reshape(B_LOC, L, D) for o in outs],
                          axis=0).astype(np.float32)
